# version 05
# baseline (speedup 1.0000x reference)
import math
import numpy as np
import ml_dtypes
import concourse.bass as bass
import concourse.mybir as mybir
from concourse.bass_utils import run_bass_kernel_spmd

F32 = mybir.dt.float32
BF16 = mybir.dt.bfloat16
AF = mybir.ActivationFunctionType
ALU = mybir.AluOpType
AX = mybir.AxisListType.X
BF = ml_dtypes.bfloat16

T = 4096
D = 2048
HPC = 8
E = 8192
OWN = 2048
LAM0 = 0.8 - 0.6 * math.exp(-0.3 * 12)
LN_EPS = 1e-5
FFN_EPS = 1e-8
NT = T // 128
ENGINES = ("sync", "gpsimd", "scalar", "vector", "tensor")


class Sem:
    __slots__ = ("h", "n", "name")

    def __init__(self, nc, name):
        self.h = nc.semaphore(name).__enter__()
        self.n = 0
        self.name = name


SELF_ENGS = ("vector", "scalar")


class Prog:

    def __init__(self, nc):
        self.nc = nc
        self.ops = {e: [] for e in ENGINES}
        self.sems = {}
        self.waited = {}
        self.marker = {e: None for e in SELF_ENGS}

    def sem(self, name):
        if name not in self.sems:
            self.sems[name] = Sem(self.nc, name)
        return self.sems[name]

    def add(self, eng, fn, waits=(), inc=None, inc_k=1, dma=False):
        w = []
        if eng in SELF_ENGS and not dma and self.marker[eng] is not None:
            w.append(self.marker[eng])
        for s, v in waits:
            if v <= 0:
                continue
            key = (eng, s.name)
            if self.waited.get(key, -1) >= v:
                continue
            self.waited[key] = v
            w.append((s, v))
        n = None
        if inc is None and eng in SELF_ENGS and not dma:
            inc = self.sem("_self_" + eng)
        if inc is not None:
            inc.n += inc_k
            n = inc.n
        if eng in SELF_ENGS and not dma:
            self.marker[eng] = (inc, n)
        self.ops[eng].append((fn, w, inc, inc_k))
        return n

    def emit_block(self):
        with self.nc.Block() as blk:
            for eng in ENGINES:
                ops = self.ops[eng]
                if not ops:
                    continue

                def run(h, ops=ops):
                    for fn, w, inc, k in ops:
                        for s, v in w:
                            h.wait_ge(s.h, v)
                        inst = fn(h)
                        if inc is not None and inst is not None:
                            inst.then_inc(inc.h, k)

                getattr(blk, eng)(run)
        self.ops = {e: [] for e in ENGINES}
        self.waited = {}


def build_nc(taps=(), phases=4, d_mode="full"):
    nc = bass.Bass("TRN2", target_bir_lowering=False)

    def dram(name, shape, dtype, kind="Internal"):
        k = "ExternalOutput" if (name in taps and kind == "Internal") else kind
        return nc.dram_tensor(name, shape, dtype, kind=k)

    x_d = nc.dram_tensor("x", [T, D], F32, kind="ExternalInput")
    wq_d = nc.dram_tensor("wq", [D, 1024], F32, kind="ExternalInput")
    wk_d = nc.dram_tensor("wk", [D, 1024], F32, kind="ExternalInput")
    wv_d = nc.dram_tensor("wv", [D, 1024], F32, kind="ExternalInput")
    wo_d = nc.dram_tensor("wo", [D, D], F32, kind="ExternalInput")
    lams_d = nc.dram_tensor("lams", [1, 256], F32, kind="ExternalInput")
    wu_d = nc.dram_tensor("wu", [E, D], F32, kind="ExternalInput")
    wg_d = nc.dram_tensor("wg", [E, D], F32, kind="ExternalInput")
    wd_d = nc.dram_tensor("wd", [D, E], F32, kind="ExternalInput")
    ident_d = nc.dram_tensor("ident", [128, 128], BF16, kind="ExternalInput")
    identf_d = nc.dram_tensor("identf", [128, 128], F32, kind="ExternalInput")
    onescol_d = nc.dram_tensor("onescol", [128, 1], BF16, kind="ExternalInput")
    onescolf_d = nc.dram_tensor("onescolf", [128, 1], F32, kind="ExternalInput")
    onesrowf_d = nc.dram_tensor("onesrowf", [1, 128], F32, kind="ExternalInput")
    masks_d = nc.dram_tensor("masks", [128, 2048], BF16, kind="ExternalInput")
    epsb_d = nc.dram_tensor("epsb", [128, 1], F32, kind="ExternalInput")
    xown_d = nc.dram_tensor("xown", [OWN, D], F32, kind="ExternalInput")
    out_d = nc.dram_tensor("out", [OWN, D], F32, kind="ExternalOutput")

    kt_scr = dram("kt_scr", [HPC, 128, T], BF16)
    qt_scr = dram("qt_scr", [HPC, 128, T], BF16)
    v_scr = dram("v_scr", [HPC, 128, T], BF16)
    y_scr = dram("y_scr", [OWN, D], F32)
    x1_scr = dram("x1_scr", [OWN, D], F32)
    ln2_scr = dram("ln2_scr", [16, 128, OWN], BF16)
    wu_nat = dram("wu_nat", [E, D], BF16)
    wg_nat = dram("wg_nat", [E, D], BF16)
    wd_nat = dram("wd_nat", [D, E], BF16)
    wu_T = dram("wu_T", [D, E], BF16)
    wg_T = dram("wg_T", [D, E], BF16)
    wd_T = dram("wd_T", [E, D], BF16)

    scope_stack = []

    def sb(name, shape, dtype):
        g = nc.sbuf_tensor(name, shape, dtype)
        t = g.__enter__()
        scope_stack[-1].append(g)
        return t

    def pstensor(name, shape, dtype):
        g = nc.psum_tensor(name, shape, dtype)
        t = g.__enter__()
        scope_stack[-1].append(g)
        return t

    def push_scope():
        scope_stack.append([])

    def pop_scope():
        for g in reversed(scope_stack.pop()):
            g.__exit__(None, None, None)

    push_scope()
    ident = sb("c_ident", [128, 128], BF16)
    identf = sb("c_identf", [128, 128], F32)
    onescol = sb("c_onescol", [128, 1], BF16)
    onescolf = sb("c_onescolf", [128, 1], F32)
    onesrowf = sb("c_onesrowf", [1, 128], F32)
    masks = sb("c_masks", [128, 2048], BF16)
    lams = sb("c_lams", [1, 256], F32)
    lamt = sb("c_lamt", [1, 64], F32)
    lamrow = sb("c_lamrow", [1, 4], F32)
    csfin = sb("c_csfin", [1, 16], F32)
    negB = sb("c_negB", [128, 16], F32)
    qscal = sb("c_qscal", [128, 9], F32)
    maxk2 = sb("c_maxk2", [1, 16], F32)
    maxq2 = sb("c_maxq2", [1, 16], F32)
    qacc = sb("c_qacc", [128, 3], F32)
    srow = sb("c_srow", [1, 9], F32)
    csr = sb("c_csr", [1, 2], F32)
    epsb = sb("c_epsb", [128, 1], F32)

    p = Prog(nc)
    SM = p.sem

    push_scope()
    xb = [sb(f"a_x{i}", [128, D], F32) for i in range(2)]
    sq_scr = sb("a_sq", [128, D], F32)
    ab_m = [sb(f"a_m{i}", [128, 1], F32) for i in range(2)]
    ab_s = [sb(f"a_s{i}", [128, 1], F32) for i in range(2)]
    ab_t = [sb(f"a_t{i}", [128, 1], F32) for i in range(2)]
    ab_v = [sb(f"a_v{i}", [128, 1], F32) for i in range(2)]
    ab_sd = [sb(f"a_sd{i}", [128, 1], F32) for i in range(2)]
    ab_r = [sb(f"a_r{i}", [128, 1], F32) for i in range(2)]
    ln1b = [sb(f"a_ln1b{i}", [128, D], BF16) for i in range(2)]
    ln1T = [sb(f"a_ln1T{i}", [128, 16 * 512], BF16) for i in range(2)]
    wcol = [sb(f"a_wcol{i}", [128, 16 * 128], BF16) for i in range(2)]
    pstg = [sb(f"a_pstg{i}", [128, 512], BF16) for i in range(4)]
    vstg = sb("a_vstg", [128, 512], BF16)
    k2s = sb("a_k2", [128, 512], F32)
    qw = [sb(f"a_qw{i}", [128, D], F32) for i in range(2)]
    qa_t = [sb(f"a_qa{i}", [128, 1], F32) for i in range(2)]

    psTR = pstensor("pA_tr", [128, 2048], BF16)
    psP = [pstensor(f"pA_p{i}", [128, 512], F32) for i in range(3)]
    psV = pstensor("pA_v", [128, 512], BF16)
    psC = pstensor("pA_cs", [33, 512], F32)
    psK = pstensor("pA_k", [128, 16], F32)

    aC = SM("aC"); aX = SM("aX"); aSQ = SM("aSQ"); aV_ = SM("aV"); aSD = SM("aSD")
    aLN = SM("aLN"); aTR = SM("aTR"); aTC = SM("aTC"); aW = SM("aW"); aPJ = SM("aPJ")
    aEV = SM("aEV"); aVT = SM("aVT"); aVC = SM("aVC"); aSP = SM("aSP"); aVSP = SM("aVSP")
    aQ1 = SM("aQ1"); aQA = SM("aQA"); aQS = SM("aQS"); aCSm = SM("aCSm")
    aCSr = SM("aCSr"); aLAM = SM("aLAM"); aNB = SM("aNB"); aMX = SM("aMX")

    for dst, src in ((ident, ident_d), (identf, identf_d), (onescol, onescol_d),
                     (onescolf, onescolf_d), (onesrowf, onesrowf_d),
                     (masks, masks_d), (lams, lams_d), (epsb, epsb_d)):
        p.add("sync", lambda h, dst=dst, src=src: h.dma_start(out=dst[:, :], in_=src.ap()),
              inc=aC, inc_k=16)
    CL = aC.n

    aMS = SM("aMS")
    p.add("gpsimd", lambda h: h.memset(maxk2[:, :], 0.0), inc=aMS)
    p.add("gpsimd", lambda h: h.memset(maxq2[:, :], 0.0), inc=aMS)
    p.add("gpsimd", lambda h: h.memset(qacc[:, :], 0.0), inc=aQA)

    p.add("vector", lambda h: h.tensor_mul(lamt[0:1, :], lams[0:1, 0:64], lams[0:1, 64:128]),
          waits=[(aC, CL)])
    p.add("vector", lambda h: h.reduce_sum(csr[0:1, 0:1], lamt[0:1, :], axis=AX))
    p.add("vector", lambda h: h.tensor_mul(lamt[0:1, :], lams[0:1, 128:192], lams[0:1, 192:256]))
    p.add("vector", lambda h: h.reduce_sum(csr[0:1, 1:2], lamt[0:1, :], axis=AX), inc=aLAM)
    p.add("scalar", lambda h: h.activation(out=csr[0:1, 0:2], in_=csr[0:1, 0:2], func=AF.Exp),
          waits=[(aLAM, 1), (aC, CL)], inc=aLAM)
    p.add("vector", lambda h: h.tensor_sub(lamrow[0:1, 0:1], csr[0:1, 0:1], csr[0:1, 1:2]),
          waits=[(aLAM, 2)])
    p.add("vector", lambda h: h.tensor_scalar(lamrow[0:1, 0:1], lamrow[0:1, 0:1],
                                              float(LAM0), None, ALU.add))
    p.add("vector", lambda h: h.tensor_scalar(lamrow[0:1, 1:2], lamrow[0:1, 0:1],
                                              -1.0, None, ALU.mult), inc=aLAM)

    def q_chunks(srcs):
        for mi, wsrc in enumerate(srcs):
            if mi < 2:
                for r in range(E // 128):
                    yield mi, wsrc.ap()[r * 128:(r + 1) * 128, :]
            else:
                for r in range(D // 128):
                    for j in range(4):
                        yield mi, wsrc.ap()[r * 128:(r + 1) * 128, j * 2048:(j + 1) * 2048]

    q1list = list(q_chunks((wu_d, wg_d, wd_d)))
    NQ = len(q1list)
    for ci, (mi, ap) in enumerate(q1list):
        p.add("gpsimd", lambda h, ap=ap, ci=ci: h.dma_start(out=qw[ci % 2][:, :], in_=ap),
              waits=[(aQA, ci)], inc=aQ1, inc_k=16)
        p.add("scalar", lambda h, ci=ci: h.activation(
            out=sq_scr[:, :], in_=qw[ci % 2][:, :], func=AF.Abs, accum_out=qa_t[ci % 2][:, :]),
            waits=[(aQ1, 16 * (ci + 1)), (aQS, ci - 1)], inc=aQA)
        p.add("vector", lambda h, mi=mi, ci=ci: h.tensor_add(
            qacc[:, mi:mi + 1], qacc[:, mi:mi + 1], qa_t[ci % 2][:, :]),
            waits=[(aQA, ci + 2)], inc=aQS)

    def ln_tile(i):
        bi = i % 2
        p.add("sync", lambda h, i=i, bi=bi: h.dma_start(
            out=xb[bi][:, :], in_=x_d.ap()[i * 128:(i + 1) * 128, :]),
            waits=[(aLN, i - 1)], inc=aX, inc_k=16)
        p.add("vector", lambda h, bi=bi: h.reduce_sum(ab_m[bi][:, :], xb[bi][:, :], axis=AX),
              waits=[(aX, 16 * (i + 1))])
        p.add("scalar", lambda h, bi=bi: h.activation(
            out=sq_scr[:, :], in_=xb[bi][:, :], func=AF.Square, accum_out=ab_s[bi][:, :]),
            waits=[(aX, 16 * (i + 1))], inc=aSQ)
        p.add("vector", lambda h, bi=bi: h.tensor_scalar(
            ab_m[bi][:, :], ab_m[bi][:, :], 1.0 / D, None, ALU.mult))
        p.add("vector", lambda h, bi=bi: h.tensor_mul(ab_t[bi][:, :], ab_m[bi][:, :], ab_m[bi][:, :]))
        p.add("vector", lambda h, bi=bi: h.scalar_tensor_tensor(
            ab_v[bi][:, :], ab_s[bi][:, :], 1.0 / D, ab_t[bi][:, :], ALU.mult, ALU.subtract),
            waits=[(aSQ, i + 1)], inc=aV_)
        p.add("scalar", lambda h, bi=bi: h.activation(
            out=ab_sd[bi][:, :], in_=ab_v[bi][:, :], func=AF.Sqrt, bias=epsb[:, :]),
            waits=[(aV_, i + 1)], inc=aSD)
        p.add("vector", lambda h, bi=bi: h.reciprocal(ab_r[bi][:, :], ab_sd[bi][:, :]),
              waits=[(aSD, i + 1)])
        p.add("vector", lambda h, bi=bi: h.tensor_scalar(
            ln1b[bi][:, :], xb[bi][:, :], ab_m[bi][:, :], ab_r[bi][:, :],
            ALU.subtract, ALU.mult),
            waits=[(aTR, i - 1)], inc=aLN)

        def trs(h, bi=bi):
            last = None
            for j in range(16):
                last = h.transpose(psTR[:, j * 128:(j + 1) * 128],
                                   ln1b[bi][:, j * 128:(j + 1) * 128], ident[:, :])
            return last
        p.add("tensor", trs, waits=[(aLN, i + 1), (aTC, i), (aC, CL)], inc=aTR)
        c2 = (i // 4) % 2
        t = i % 4
        dst = ln1T[c2].ap().rearrange("q (j w) -> q j w", w=512)[:, :, t * 128:(t + 1) * 128]
        src = psTR.ap().rearrange("q (j w) -> q j w", w=128)
        p.add("vector", lambda h, dst=dst, src=src: h.tensor_copy(dst, src),
              waits=[(aTR, i + 1), (aPJ, ((i // 4) - 1) * 24)], inc=aTC)

    cons = {}

    def proj_chunk(c):
        for pi, (wsrc, scr) in enumerate(((wq_d, qt_scr), (wk_d, kt_scr), (wv_d, v_scr))):
            for do in range(8):
                g = (c * 3 + pi) * 8 + do
                for dc in range(16):
                    p.add("gpsimd", lambda h, wsrc=wsrc, do=do, dc=dc, g=g: h.dma_start(
                        out=wcol[g % 2][:, dc * 128:(dc + 1) * 128],
                        in_=wsrc.ap()[dc * 128:(dc + 1) * 128, do * 128:(do + 1) * 128]),
                        waits=[(aPJ, g - 1)], inc=aW, inc_k=16)

                def mms(h, g=g, c=c):
                    last = None
                    for dc in range(16):
                        last = h.matmul(psP[g % 3][:, :], wcol[g % 2][:, dc * 128:(dc + 1) * 128],
                                        ln1T[c % 2][:, dc * 512:(dc + 1) * 512],
                                        start=(dc == 0), stop=(dc == 15))
                    return last
                p.add("tensor", mms,
                      waits=[(aW, 256 * (g + 1)), (aTC, 4 * (c + 1)), (aEV, g - 2)], inc=aPJ)
                p.add("vector", lambda h, g=g: h.tensor_copy(pstg[g % 4][:, :], psP[g % 3][:, :]),
                      waits=[(aPJ, g + 1)] + cons.get(g - 4, []), inc=aEV)
                if pi < 2:
                    sp = p.add("sync", lambda h, scr=scr, do=do, c=c, g=g: h.dma_start(
                        out=scr.ap()[do, :, c * 512:(c + 1) * 512], in_=pstg[g % 4][:, :]),
                        waits=[(aEV, g + 1)], inc=aSP, inc_k=16)
                    p.add("scalar", lambda h, g=g: h.activation(
                        out=k2s[:, :], in_=pstg[g % 4][:, :], func=AF.Square),
                        waits=[(aEV, g + 1), (aCSm, aCSm.n)], inc=aCSm)

                    def csmm(h):
                        h.matmul(psC[0:1, :], onescolf[0:64, :], k2s[0:64, :],
                                 start=True, stop=True)
                        return h.matmul(psC[32:33, :], onescolf[64:128, :], k2s[64:128, :],
                                        start=True, stop=True, tile_position=(64, 32))
                    sqv = p.add("tensor", csmm, waits=[(aCSm, aCSm.n), (aCSr, aCSr.n)],
                                inc=aCSm)
                    mx = maxk2 if pi == 1 else maxq2
                    p.add("vector", lambda h: h.reduce_max(csr[0:1, 0:1], psC[0:1, :], axis=AX),
                          waits=[(aCSm, sqv)])
                    p.add("vector", lambda h: h.reduce_max(csr[0:1, 1:2], psC[32:33, :], axis=AX))
                    p.add("vector", lambda h, mx=mx, do=do: h.tensor_max(
                        mx[0:1, 2 * do:2 * do + 1], mx[0:1, 2 * do:2 * do + 1], csr[0:1, 0:1]),
                        waits=[(aMS, 2)])
                    p.add("vector", lambda h, mx=mx, do=do: h.tensor_max(
                        mx[0:1, 2 * do + 1:2 * do + 2], mx[0:1, 2 * do + 1:2 * do + 2],
                        csr[0:1, 1:2]), inc=aCSr)
                    cons[g] = [(aSP, sp), (aCSm, sqv)]
                else:
                    def vtr(h, g=g):
                        last = None
                        for j in range(4):
                            last = h.transpose(psV[:, j * 128:(j + 1) * 128],
                                               pstg[g % 4][:, j * 128:(j + 1) * 128],
                                               ident[:, :])
                        return last
                    vt = p.add("tensor", vtr, waits=[(aEV, g + 1), (aVC, aVC.n)], inc=aVT)
                    p.add("vector", lambda h: h.tensor_copy(vstg[:, :], psV[:, :]),
                          waits=[(aVT, vt), (aVSP, aVSP.n)], inc=aVC)
                    vs = p.add("sync", lambda h, do=do, c=c: h.dma_start(
                        out=v_scr.ap()[do, :, c * 512:(c + 1) * 512], in_=vstg[:, :]),
                        waits=[(aVC, aVC.n)], inc=aVSP, inc_k=16)
                    cons[g] = [(aVT, vt)]

    for c in range(8):
        for t in range(4):
            ln_tile(4 * c + t)
        proj_chunk(c)
    ASP_END = aSP.n
    AVSP_END = aVSP.n

    p.add("tensor", lambda h: h.matmul(psC[0:1, 0:3], onescolf[:, :], qacc[:, :],
                                       start=True, stop=True),
          waits=[(aQS, NQ), (aCSr, aCSr.n), (aCSm, aCSm.n)], inc=aQS)
    p.add("vector", lambda h: h.tensor_scalar(
        srow[0:1, 0:3], psC[0:1, 0:3], 1.0 / (E * D), float(FFN_EPS), ALU.mult, ALU.add),
        waits=[(aQS, NQ + 1)])
    p.add("vector", lambda h: h.tensor_scalar(srow[0:1, 3:6], srow[0:1, 0:3], 0.5, None, ALU.mult))
    p.add("vector", lambda h: h.tensor_scalar(srow[0:1, 6:9], srow[0:1, 0:3], -0.5, None, ALU.mult),
          inc=aQS)
    p.add("tensor", lambda h: h.matmul(psK[:, 0:9], onesrowf[:, :], srow[0:1, :],
                                       start=True, stop=True),
          waits=[(aQS, NQ + 2)], inc=aQS)
    p.add("vector", lambda h: h.tensor_copy(qscal[:, :], psK[:, 0:9]),
          waits=[(aQS, NQ + 3)], inc=aQS)
    p.add("vector", lambda h: h.tensor_mul(csfin[0:1, :], maxk2[0:1, :], maxq2[0:1, :]),
          waits=[(aCSr, 128)], inc=aMX)
    p.add("scalar", lambda h: h.activation(out=csfin[0:1, :], in_=csfin[0:1, :], func=AF.Sqrt),
          waits=[(aMX, 1)], inc=aMX)
    p.add("vector", lambda h: h.tensor_scalar(csfin[0:1, :], csfin[0:1, :], -0.125, None, ALU.mult),
          waits=[(aMX, 2)], inc=aMX)
    p.add("tensor", lambda h: h.matmul(psK[:, 0:16], onesrowf[:, :], csfin[0:1, :],
                                       start=True, stop=True),
          waits=[(aMX, 3)], inc=aMX)
    p.add("vector", lambda h: h.tensor_copy(negB[:, :], psK[:, 0:16]),
          waits=[(aMX, 4)], inc=aNB)

    if phases == 1:
        p.add("sync", lambda h: None,
              waits=[(aSP, ASP_END), (aVSP, AVSP_END), (aNB, 1), (aQS, aQS.n)])
    p.emit_block()
    pop_scope()
    if phases == 1:
        pop_scope()
        return nc

    push_scope()
    OnT = sb("bc_OnT", [128, HPC * T], BF16)

    push_scope()
    KTh = [sb(f"b_kt{i}", [128, T], BF16) for i in range(2)]
    QTh = [sb(f"b_qt{i}", [128, T], BF16) for i in range(2)]
    Vh = [sb(f"b_v{i}", [128, T], BF16) for i in range(2)]
    Ebuf = [sb(f"b_e{i}", [128, 2048], BF16) for i in range(2)]
    s0r = sb("b_s0r", [1, 512], F32)
    s1r = sb("b_s1r", [1, 512], F32)
    O0c = sb("b_o0", [128, 512], F32)
    O1c = sb("b_o1", [128, 512], F32)
    bc0s = sb("b_bc0", [128, 512], F32)
    bc1s = sb("b_bc1", [128, 512], F32)
    Ofb = sb("b_of", [128, 512], F32)
    of2 = sb("b_of2", [128, 512], F32)
    gr0 = sb("b_gr0", [1, 512], F32)
    gr1 = sb("b_gr1", [1, 512], F32)
    grow = sb("b_grow", [1, 512], F32)
    grow2 = sb("b_grow2", [1, 512], F32)
    qw2 = sb("b_qw", [128, D], F32)
    g2b = sb("b_g2", [128, D], BF16)
    tern = [sb(f"b_tern{i}", [128, D], BF16) for i in range(2)]

    psS0 = pstensor("pB_s0", [128, 1024], F32)
    psS1 = pstensor("pB_s1", [128, 1024], F32)
    psO0 = pstensor("pB_O0", [128, 512], F32)
    psO1 = pstensor("pB_O1", [128, 512], F32)
    psSm = pstensor("pB_sm", [33, 512], F32)
    psSp = pstensor("pB_sp", [128, 512], F32)

    bLD = SM("bLD"); bSC = SM("bSC"); bEXP = SM("bEXP"); bMSK = SM("bMSK")
    bAV = SM("bAV"); bOF = SM("bOF"); bBC = SM("bBC"); bGN = SM("bGN"); bON = SM("bON")
    bQ2 = SM("bQ2"); bQD = SM("bQD"); bQSP = SM("bQSP")

    q2list = list(q_chunks((wu_d, wg_d, wd_d)))
    nat_aps = list(q_chunks((wu_nat, wg_nat, wd_nat)))
    q2_iter = iter(range(len(q2list)))

    def emit_q2(n):
        for _ in range(n):
            try:
                ci = next(q2_iter)
            except StopIteration:
                return
            mi, src = q2list[ci]
            _, dst = nat_aps[ci]
            p.add("gpsimd", lambda h, src=src: h.dma_start(out=qw2[:, :], in_=src),
                  waits=[(bQD, 2 * ci), (aNB, 1)], inc=bQ2, inc_k=16)
            p.add("vector", lambda h, mi=mi: h.tensor_scalar(
                g2b[:, :], qw2[:, :], qscal[:, 3 + mi:4 + mi], 1.0, ALU.is_ge, ALU.subtract),
                waits=[(bQ2, 16 * (ci + 1))], inc=bQD)
            p.add("vector", lambda h, mi=mi, ci=ci: h.scalar_tensor_tensor(
                tern[ci % 2][:, :], qw2[:, :], qscal[:, 6 + mi:7 + mi], g2b[:, :],
                ALU.is_gt, ALU.add),
                waits=[(bQSP, (ci - 1) * 16)], inc=bQD)
            p.add("gpsimd", lambda h, dst=dst, ci=ci: h.dma_start(out=dst, in_=tern[ci % 2][:, :]),
                  waits=[(bQD, 2 * (ci + 1))], inc=bQSP, inc_k=16)

    def slot(hl, s):
        hb = hl % 2
        qs = (s - 1) * 512
        nkt = 4 * s
        hh0 = 2 * hl
        prev_of = bOF.n
        for g in range(2 * s):
            def scores(h, g=g, hb=hb, qs=qs):
                last = None
                for (pS, r0) in ((psS0, 0), (psS1, 64)):
                    for t2 in range(2):
                        kt = 2 * g + t2
                        last = h.matmul(pS[:, t2 * 512:(t2 + 1) * 512],
                                        KTh[hb][r0:r0 + 64, kt * 128:(kt + 1) * 128],
                                        QTh[hb][r0:r0 + 64, qs:qs + 512],
                                        start=True, stop=True)
                return last
            p.add("tensor", scores,
                  waits=[(bLD, 48 * (hl + 1)), (bEXP, bEXP.n), (aNB, 1)], inc=bSC)
            p.add("scalar", lambda h, g=g: h.activation(
                out=Ebuf[g % 2][:, 0:1024], in_=psS0[:, :], func=AF.Exp,
                scale=0.125, bias=negB[:, hh0:hh0 + 1]),
                waits=[(bSC, bSC.n), (bAV, bAV.n - 1)], inc=bEXP)
            p.add("scalar", lambda h, g=g: h.activation(
                out=Ebuf[g % 2][:, 1024:2048], in_=psS1[:, :], func=AF.Exp,
                scale=0.125, bias=negB[:, hh0 + 1:hh0 + 2]),
                inc=bEXP)
            for t2 in range(2):
                kt = 2 * g + t2
                dix = kt - (nkt - 4)
                if 0 <= dix < 4:
                    for half in range(2):
                        o = half * 1024 + t2 * 512
                        p.add("vector", lambda h, g=g, o=o, dix=dix: h.tensor_mul(
                            Ebuf[g % 2][:, o:o + 512], Ebuf[g % 2][:, o:o + 512],
                            masks[:, dix * 512:(dix + 1) * 512]),
                            waits=[(bEXP, bEXP.n)], inc=bMSK)

            def sumav(h, g=g, hl=hl, hb=hb, nkt=nkt):
                last = None
                for t2 in range(2):
                    kt = 2 * g + t2
                    st = (kt == 0)
                    sp = (kt == nkt - 1)
                    e0 = Ebuf[g % 2][:, t2 * 512:(t2 + 1) * 512]
                    e1 = Ebuf[g % 2][:, 1024 + t2 * 512:1024 + (t2 + 1) * 512]
                    h.matmul(psSm[0:1, :], onescol[:, :], e0, start=st, stop=sp)
                    h.matmul(psSm[32:33, :], onescol[:, :], e1, start=st, stop=sp,
                             tile_position=(0, 32))
                    vt = Vh[hb][:, kt * 128:(kt + 1) * 128]
                    h.matmul(psO0[:, :], vt, e0, start=st, stop=sp)
                    last = h.matmul(psO1[:, :], vt, e1, start=st, stop=sp)
                return last
            p.add("tensor", sumav,
                  waits=[(bEXP, bEXP.n), (bMSK, bMSK.n)] +
                        ([(bOF, prev_of)] if g == 0 else []),
                  inc=bAV)
        p.add("vector", lambda h: h.tensor_copy(s0r[:, :], psSm[0:1, :]), waits=[(bAV, bAV.n)])
        p.add("vector", lambda h: h.tensor_copy(s1r[:, :], psSm[32:33, :]))
        p.add("vector", lambda h: h.tensor_copy(O0c[:, :], psO0[:, :]))
        p.add("vector", lambda h: h.tensor_copy(O1c[:, :], psO1[:, :]), inc=bOF)
        p.add("vector", lambda h: h.reciprocal(grow[0:1, :], s0r[0:1, :]))
        p.add("vector", lambda h: h.reciprocal(grow2[0:1, :], s1r[0:1, :]))
        p.add("vector", lambda h: h.tensor_single_scalar(
            grow2[0:1, :], grow2[0:1, :], lamrow[0:1, 1:2], ALU.mult), inc=bOF)
        p.add("tensor", lambda h: h.matmul(psSp[:, :], onesrowf[:, :], grow[0:1, :],
                                           start=True, stop=True),
              waits=[(bOF, bOF.n), (bBC, bBC.n)], inc=bBC)
        p.add("vector", lambda h: h.tensor_copy(bc0s[:, :], psSp[:, :]),
              waits=[(bBC, bBC.n)], inc=bBC)
        p.add("tensor", lambda h: h.matmul(psSp[:, :], onesrowf[:, :], grow2[0:1, :],
                                           start=True, stop=True),
              waits=[(bBC, bBC.n)], inc=bBC)
        p.add("vector", lambda h: h.tensor_copy(bc1s[:, :], psSp[:, :]),
              waits=[(bBC, bBC.n)], inc=bBC)
        p.add("vector", lambda h: h.tensor_mul(Ofb[:, :], O0c[:, :], bc0s[:, :]))
        p.add("vector", lambda h: h.tensor_mul(bc1s[:, :], O1c[:, :], bc1s[:, :]))
        p.add("vector", lambda h: h.tensor_add(Ofb[:, :], Ofb[:, :], bc1s[:, :]), inc=bOF)
        p.add("vector", lambda h: h.tensor_mul(of2[:, :], Ofb[:, :], Ofb[:, :]), inc=bGN)

        def gsmm(h):
            h.matmul(psSp[0:1, :], onescolf[:, :], Ofb[:, :], start=True, stop=True)
            return h.matmul(psSp[32:33, :], onescolf[:, :], of2[:, :], start=True, stop=True,
                            tile_position=(0, 32))
        p.add("tensor", gsmm, waits=[(bGN, bGN.n), (bBC, bBC.n)], inc=bBC)
        p.add("vector", lambda h: h.tensor_copy(gr0[:, :], psSp[0:1, :]),
              waits=[(bBC, bBC.n)])
        p.add("vector", lambda h: h.tensor_copy(gr1[:, :], psSp[32:33, :]), inc=bBC)
        p.add("vector", lambda h: h.tensor_scalar(grow[0:1, :], gr0[0:1, :], 1.0 / 128,
                                                  None, ALU.mult))
        p.add("vector", lambda h: h.tensor_mul(grow2[0:1, :], grow[0:1, :], grow[0:1, :]))
        p.add("vector", lambda h: h.scalar_tensor_tensor(
            grow2[0:1, :], gr1[0:1, :], 1.0 / 128, grow2[0:1, :], ALU.mult, ALU.subtract),
            inc=bGN)
        p.add("scalar", lambda h: h.activation(out=grow2[0:1, :], in_=grow2[0:1, :],
                                               func=AF.Sqrt, bias=epsb[0:1, :]),
              waits=[(bGN, bGN.n)], inc=bGN)
        p.add("vector", lambda h: h.reciprocal(grow2[0:1, :], grow2[0:1, :]),
              waits=[(bGN, bGN.n)])
        p.add("vector", lambda h: h.tensor_scalar(
            grow2[0:1, :], grow2[0:1, :], float(1.0 - LAM0), None, ALU.mult))
        p.add("vector", lambda h: h.tensor_mul(grow[0:1, :], grow[0:1, :], grow2[0:1, :]),
              inc=bGN)
        p.add("tensor", lambda h: h.matmul(psSp[:, :], onesrowf[:, :], grow2[0:1, :],
                                           start=True, stop=True),
              waits=[(bGN, bGN.n), (bBC, bBC.n)], inc=bBC)
        p.add("vector", lambda h: h.tensor_copy(bc0s[:, :], psSp[:, :]),
              waits=[(bBC, bBC.n)], inc=bBC)
        p.add("tensor", lambda h: h.matmul(psSp[:, :], onesrowf[:, :], grow[0:1, :],
                                           start=True, stop=True),
              waits=[(bBC, bBC.n)], inc=bBC)
        p.add("vector", lambda h: h.tensor_copy(bc1s[:, :], psSp[:, :]),
              waits=[(bBC, bBC.n)], inc=bBC)
        p.add("vector", lambda h: h.tensor_mul(Ofb[:, :], Ofb[:, :], bc0s[:, :]))
        p.add("vector", lambda h, hl=hl, qs=qs: h.tensor_sub(
            OnT[:, hl * T + qs:hl * T + qs + 512], Ofb[:, :], bc1s[:, :]), inc=bON)

    for hl in range(HPC):
        hb = hl % 2
        p.add("gpsimd", lambda h, hl=hl, hb=hb: h.dma_start(
            out=KTh[hb][:, :], in_=kt_scr.ap()[hl, :, :]),
            waits=[(aSP, ASP_END), (aVSP, AVSP_END), (bON, (hl - 1) * 8)], inc=bLD, inc_k=16)
        p.add("gpsimd", lambda h, hl=hl, hb=hb: h.dma_start(
            out=QTh[hb][:, :], in_=qt_scr.ap()[hl, :, :]), inc=bLD, inc_k=16)
        p.add("gpsimd", lambda h, hl=hl, hb=hb: h.dma_start(
            out=Vh[hb][:, :], in_=v_scr.ap()[hl, :, :]), inc=bLD, inc_k=16)
        for s in range(1, 9):
            slot(hl, s)
        emit_q2(24)
    emit_q2(len(q2list))
    QSP_ALL = 16 * len(q2list)

    if phases == 2:
        p.add("sync", lambda h: None, waits=[(bQSP, QSP_ALL), (bON, 64)])
        if "OnT" in taps:
            ont_d = nc.dram_tensor("OnT", [128, HPC * T], BF16, kind="ExternalOutput")
            for j in range(8):
                p.add("sync", lambda h, j=j: h.dma_start(
                    out=ont_d.ap()[:, j * T:(j + 1) * T], in_=OnT[:, j * T:(j + 1) * T]),
                    waits=[(bON, 64)], inc=bLD, inc_k=16)
            p.add("sync", lambda h: None, waits=[(bLD, bLD.n)])
    p.emit_block()
    pop_scope()
    if phases == 2:
        pop_scope()
        pop_scope()
        return nc

    push_scope()
    wosb = [sb(f"c_wo{i}", [128, 2048], BF16) for i in range(2)]
    ystg = sb("c_ystg", [128, 2048], F32)
    ytst = [sb(f"c_ytst{i}", [128, 512], F32) for i in range(2)]
    tstg = [sb(f"c_tstg{i}", [128, 8192], BF16) for i in range(2)]
    yb = sb("c_yb", [128, D], F32)
    xb2 = sb("c_xb", [128, D], F32)
    x1b = sb("c_x1b", [128, D], F32)
    ln2b = sb("c_ln2b", [128, D], BF16)
    l2st = sb("c_l2st", [128, 2048], BF16)
    sq2 = sb("c_sq2", [128, D], F32)
    cb = [sb(f"c_cb{i}", [128, 1], F32) for i in range(5)]
    cb2 = [sb(f"c_cb2{i}", [128, 1], F32) for i in range(5)]

    psY = [pstensor(f"pC_y{i}", [128, 512], F32) for i in range(4)]
    psT = [pstensor(f"pC_t{i}", [128, 512], F32) for i in range(2)]
    psTRC = pstensor("pC_tr", [128, 2048], BF16)

    cW = SM("cW"); cMM = SM("cMM"); cEV = SM("cEV"); cTR = SM("cTR"); cTC = SM("cTC")
    cYSP = SM("cYSP"); cY = SM("cY"); cLN = SM("cLN"); cX1 = SM("cX1")
    cX1SP = SM("cX1SP"); cL2SP = SM("cL2SP"); cTB = SM("cTB"); cTS = SM("cTS")
    cTD = SM("cTD"); cTW = SM("cTW")

    for dc in range(16):
        for ch in range(16):
            p.add("gpsimd", lambda h, dc=dc, ch=ch: h.dma_start(
                out=wosb[dc % 2][:, ch * 128:(ch + 1) * 128],
                in_=wo_d.ap()[ch * 128:(ch + 1) * 128, dc * 128:(dc + 1) * 128]),
                waits=[(cEV, (dc - 1) * 4), (bON, 64)], inc=cW, inc_k=16)

        def yomm(h, dc=dc):
            last = None
            for hl in range(8):
                for r in range(16):
                    last = h.matmul(
                        psY[hl // 2][:, (hl % 2) * 256:(hl % 2 + 1) * 256],
                        wosb[dc % 2][:, r * 128:(r + 1) * 128],
                        OnT[:, hl * T + r:(hl + 1) * T:16],
                        start=(r == 0), stop=(r == 15))
            return last
        p.add("tensor", yomm, waits=[(cW, 256 * (dc + 1)), (cEV, dc * 4), (bON, 64)], inc=cMM)
        for q4 in range(4):
            p.add("vector", lambda h, q4=q4: h.tensor_copy(
                ystg[:, q4 * 512:(q4 + 1) * 512], psY[q4][:, :]),
                waits=[(cMM, cMM.n), (cTR, (dc - 1) * 4 + q4 + 1)], inc=cEV)
        for g4 in range(4):
            def ytr(h, g4=g4):
                last = None
                for j in range(4):
                    last = h.transpose(psT[g4 % 2][:, j * 128:(j + 1) * 128],
                                       ystg[:, (g4 * 4 + j) * 128:(g4 * 4 + j + 1) * 128],
                                       identf[:, :])
                return last
            gg = dc * 4 + g4
            tv = p.add("tensor", ytr, waits=[(cEV, dc * 4 + g4 + 1), (cTC, gg - 1)], inc=cTR)
            p.add("vector", lambda h, g4=g4: h.tensor_copy(ytst[g4 % 2][:, :], psT[g4 % 2][:, :]),
                  waits=[(cTR, tv), (cYSP, (gg - 1) * 64)], inc=cTC)
            for j in range(4):
                p.add("sync", lambda h, dc=dc, g4=g4, j=j: h.dma_start(
                    out=y_scr.ap()[(g4 * 4 + j) * 128:(g4 * 4 + j + 1) * 128,
                                   dc * 128:(dc + 1) * 128],
                    in_=ytst[g4 % 2][:, j * 128:(j + 1) * 128]),
                    waits=[(cTC, cTC.n)], inc=cYSP, inc_k=16)
    CYSP_END = cYSP.n

    def ln_rows(src, b, extra):
        m, s_, t_, v_, r_ = b
        p.add("vector", lambda h: h.reduce_sum(m[:, :], src[:, :], axis=AX), waits=extra)
        p.add("scalar", lambda h: h.activation(out=sq2[:, :], in_=src[:, :], func=AF.Square,
                                               accum_out=s_[:, :]), waits=extra, inc=cLN)
        p.add("vector", lambda h: h.tensor_scalar(m[:, :], m[:, :], 1.0 / D, None, ALU.mult))
        p.add("vector", lambda h: h.tensor_mul(t_[:, :], m[:, :], m[:, :]))
        p.add("vector", lambda h: h.scalar_tensor_tensor(
            v_[:, :], s_[:, :], 1.0 / D, t_[:, :], ALU.mult, ALU.subtract),
            waits=[(cLN, cLN.n)], inc=cLN)
        p.add("scalar", lambda h: h.activation(out=v_[:, :], in_=v_[:, :], func=AF.Sqrt,
                                               bias=epsb[:, :]),
              waits=[(cLN, cLN.n)], inc=cLN)
        p.add("vector", lambda h: h.reciprocal(r_[:, :], v_[:, :]), waits=[(cLN, cLN.n)])

    for qt in range(16):
        p.add("sync", lambda h, qt=qt: h.dma_start(
            out=yb[:, :], in_=y_scr.ap()[qt * 128:(qt + 1) * 128, :]),
            waits=[(cYSP, CYSP_END), (cX1, qt)], inc=cY, inc_k=16)
        p.add("sync", lambda h, qt=qt: h.dma_start(
            out=xb2[:, :], in_=xown_d.ap()[qt * 128:(qt + 1) * 128, :]),
            inc=cY, inc_k=16)
        ln_rows(yb, cb, [(cY, 32 * (qt + 1))])
        p.add("vector", lambda h: h.tensor_scalar(
            yb[:, :], yb[:, :], cb[0][:, :], cb[4][:, :], ALU.subtract, ALU.mult))
        p.add("vector", lambda h: h.tensor_add(x1b[:, :], yb[:, :], xb2[:, :]), inc=cX1)
        p.add("sync", lambda h, qt=qt: h.dma_start(
            out=x1_scr.ap()[qt * 128:(qt + 1) * 128, :], in_=x1b[:, :]),
            waits=[(cX1, cX1.n)], inc=cX1SP, inc_k=16)
        ln_rows(x1b, cb2, [(cX1, cX1.n)])
        p.add("vector", lambda h, qt=qt: h.tensor_scalar(
            ln2b[:, :], x1b[:, :], cb2[0][:, :], cb2[4][:, :], ALU.subtract, ALU.mult),
            waits=[(cTB, qt)], inc=cLN)

        def l2tr(h):
            last = None
            for j in range(16):
                last = h.transpose(psTRC[:, j * 128:(j + 1) * 128],
                                   ln2b[:, j * 128:(j + 1) * 128], ident[:, :])
            return last
        p.add("tensor", l2tr, waits=[(cLN, cLN.n), (cTS, qt)], inc=cTB)
        p.add("vector", lambda h: h.tensor_copy(
            l2st.ap().rearrange("q (j w) -> q j w", w=128),
            psTRC.ap().rearrange("q (j w) -> q j w", w=128)),
            waits=[(cTB, qt + 1), (cL2SP, (qt - 1) * 64)], inc=cTS)
        for jj in range(4):
            p.add("sync", lambda h, qt=qt, jj=jj: h.dma_start(
                out=ln2_scr.ap()[jj * 4:(jj + 1) * 4, :, qt * 128:(qt + 1) * 128]
                .rearrange("j p w -> p j w"),
                in_=l2st.ap().rearrange("q (j w) -> q j w", w=128)[:, jj * 4:(jj + 1) * 4, :]),
                waits=[(cTS, qt + 1)], inc=cL2SP, inc_k=16)

    ti = 0
    for mi, (nat, trg, nchunk, span, need) in enumerate((
            (wu_nat, wu_T, 16, E, 64), (wg_nat, wg_T, 16, E, 128),
            (wd_nat, wd_T, 64, D, 192))):
        for ch in range(nchunk):
            tb = ti % 2
            p.add("scalar", lambda h, nat=nat, ch=ch, tb=tb, span=span: h.dma_start(
                out=tstg[tb][:, 0:span], in_=nat.ap()[:, ch * 128:(ch + 1) * 128],
                transpose=True),
                waits=[(bQSP, 16 * need), (cTW, (ti - 1) * 16)], inc=cTD, inc_k=16, dma=True)
            p.add("scalar", lambda h, trg=trg, ch=ch, tb=tb, span=span: h.dma_start(
                out=trg.ap()[ch * 128:(ch + 1) * 128, :], in_=tstg[tb][:, 0:span]),
                waits=[(cTD, (ti + 1) * 16)], inc=cTW, inc_k=16, dma=True)
            ti += 1
    CTW_END = cTW.n

    if phases == 3:
        p.add("sync", lambda h: None,
              waits=[(cL2SP, 16 * 64), (cX1SP, 16 * 16), (cTW, CTW_END)])
    p.emit_block()
    pop_scope()
    pop_scope()
    if phases == 3:
        pop_scope()
        return nc

    push_scope()
    hidT = sb("d_hid", [128, 64 * 512], BF16)
    l2c = [sb(f"d_l2{i}", [128, 16 * 512], BF16) for i in range(2)]
    wub = [sb(f"d_wu{i}", [128, 16 * 256], BF16) for i in range(2)]
    wgb = [sb(f"d_wg{i}", [128, 16 * 256], BF16) for i in range(2)]
    wdb = [sb(f"d_wd{i}", [128, 1024], BF16) for i in range(2)]
    silu = sb("d_silu", [128, 512], BF16)
    zfull = sb("d_z", [128, 4 * D], F32)
    x1t = sb("d_x1", [128, D], F32)
    outb = sb("d_out", [128, D], F32)
    sq3 = sb("d_sq", [128, D], F32)
    db = [sb(f"d_b{i}", [128, 1], F32) for i in range(5)]

    dL2 = SM("dL2"); dWU = SM("dWU"); dUG = SM("dUG"); dSI = SM("dSI"); dHID = SM("dHID")
    dWD = SM("dWD"); dZ = SM("dZ"); dZE = SM("dZE"); dX1 = SM("dX1"); dLN = SM("dLN")
    dOC = SM("dOC"); sOUT = SM("sOUT")

    d_tcs = 1 if d_mode != "full" else 4
    for tc in range(d_tcs):
        for dc2 in range(16):
            p.add("sync", lambda h, tc=tc, dc2=dc2: h.dma_start(
                out=l2c[tc % 2][:, dc2 * 512:(dc2 + 1) * 512],
                in_=ln2_scr.ap()[dc2, :, tc * 512:(tc + 1) * 512]),
                waits=[(cL2SP, 16 * 64), (dUG, (tc - 1) * 64)], inc=dL2, inc_k=16)
        push_scope()
        psU = [pstensor(f"pD_u{tc}_{i}", [128, 512], F32) for i in range(2)]
        psG = [pstensor(f"pD_g{tc}_{i}", [128, 512], F32) for i in range(2)]
        hid_base = dHID.n
        si_base = dSI.n
        for eb in range(32):
            idx = tc * 32 + eb
            wi = idx % 2
            for (wsrcT, dstb) in ((wu_T, wub), (wg_T, wgb)):
                for cq in range(4):
                    p.add("gpsimd", lambda h, wsrcT=wsrcT, dstb=dstb, wi=wi, eb=eb, cq=cq:
                          h.dma_start(
                              out=dstb[wi].ap().rearrange("q (c w) -> q c w", w=256)
                              [:, cq * 4:(cq + 1) * 4, :],
                              in_=wsrcT.ap().rearrange("(c q) e -> q c e", q=128)
                              [:, cq * 4:(cq + 1) * 4, eb * 256:(eb + 1) * 256]),
                          waits=[(cTW, CTW_END), (dUG, 2 * (idx - 1))], inc=dWU, inc_k=16)
            for e2 in range(2):
                ei = eb * 2 + e2

                def ugmm(h, tc=tc, wi=wi, e2=e2, ei=ei):
                    last = None
                    for dc2 in range(16):
                        h.matmul(psU[ei % 2][:, :],
                                 wub[wi][:, dc2 * 256 + e2 * 128:dc2 * 256 + (e2 + 1) * 128],
                                 l2c[tc % 2][:, dc2 * 512:(dc2 + 1) * 512],
                                 start=(dc2 == 0), stop=(dc2 == 15))
                    for dc2 in range(16):
                        last = h.matmul(psG[ei % 2][:, :],
                                        wgb[wi][:, dc2 * 256 + e2 * 128:dc2 * 256 + (e2 + 1) * 128],
                                        l2c[tc % 2][:, dc2 * 512:(dc2 + 1) * 512],
                                        start=(dc2 == 0), stop=(dc2 == 15))
                    return last
                p.add("tensor", ugmm,
                      waits=[(dWU, 128 * (idx + 1)), (dL2, 256 * (tc + 1)),
                             (dSI, si_base + ei - 1), (dHID, hid_base + ei - 1),
                             (dZE, 8 * tc)],
                      inc=dUG)
                p.add("scalar", lambda h, ei=ei: h.activation(
                    out=silu[:, :], in_=psU[ei % 2][:, :], func=AF.Silu, scale=qscal[:, 0:1]),
                    waits=[(dUG, dUG.n), (dHID, hid_base + ei - 1)], inc=dSI)
                p.add("vector", lambda h, ei=ei: h.scalar_tensor_tensor(
                    hidT[:, ei * 512:(ei + 1) * 512], psG[ei % 2][:, :], qscal[:, 1:2],
                    silu[:, :], ALU.mult, ALU.mult),
                    waits=[(dSI, dSI.n), (dUG, dUG.n), (dZ, tc * 128)], inc=dHID)
        pop_scope()
        if d_mode == "ug0":
            break
        push_scope()
        psZ = [pstensor(f"pD_z{tc}_{i}", [128, 1024], F32) for i in range(4)]
        hid_all = dHID.n
        ze_base = dZE.n
        for half in range(2):
            for e in range(64):
                idx = tc * 128 + half * 64 + e
                wi = idx % 2
                p.add("gpsimd", lambda h, e=e, half=half, wi=wi: h.dma_start(
                    out=wdb[wi][:, :],
                    in_=wd_T.ap()[e * 128:(e + 1) * 128, half * 1024:(half + 1) * 1024]),
                    waits=[(cTW, CTW_END), (dZ, idx - 1)], inc=dWD, inc_k=16)

                def dmm(h, e=e, half=half, wi=wi):
                    last = None
                    for t2 in range(4):
                        for dd2 in range(2):
                            last = h.matmul(
                                psZ[t2][:, dd2 * 512:(dd2 + 1) * 512],
                                hidT[:, e * 512 + t2 * 128:e * 512 + (t2 + 1) * 128],
                                wdb[wi][:, dd2 * 512:(dd2 + 1) * 512],
                                start=(e == 0), stop=(e == 63))
                    return last
                p.add("tensor", dmm,
                      waits=[(dWD, 16 * (idx + 1)), (dHID, hid_all)] +
                            ([(dZE, ze_base + 4)] if (half, e) == (1, 0) else []) +
                            ([(dZE, ze_base)] if (half, e) == (0, 0) else []),
                      inc=dZ)
            for t2 in range(4):
                p.add("vector", lambda h, t2=t2, half=half: h.tensor_scalar(
                    zfull[:, t2 * D + half * 1024:t2 * D + (half + 1) * 1024],
                    psZ[t2][:, :], qscal[:, 2:3], None, ALU.mult),
                    waits=[(dZ, dZ.n), (dOC, tc * 4)], inc=dZE)
        pop_scope()
        if d_mode == "down0":
            break
        for t2 in range(4):
            row0 = tc * 512 + t2 * 128
            p.add("sync", lambda h, row0=row0: h.dma_start(
                out=x1t[:, :], in_=x1_scr.ap()[row0:row0 + 128, :]),
                waits=[(cX1SP, 16 * 16), (dOC, tc * 4 + t2 - 1)], inc=dX1, inc_k=16)
            zsrc = zfull.ap()[:, t2 * D:(t2 + 1) * D]
            p.add("vector", lambda h, zsrc=zsrc: h.reduce_sum(db[0][:, :], zsrc, axis=AX),
                  waits=[(dZE, ze_base + 8)])
            p.add("scalar", lambda h, zsrc=zsrc: h.activation(
                out=sq3[:, :], in_=zsrc, func=AF.Square, accum_out=db[1][:, :]),
                waits=[(dZE, ze_base + 8)], inc=dLN)
            p.add("vector", lambda h: h.tensor_scalar(db[0][:, :], db[0][:, :], 1.0 / D,
                                                      None, ALU.mult))
            p.add("vector", lambda h: h.tensor_mul(db[2][:, :], db[0][:, :], db[0][:, :]))
            p.add("vector", lambda h: h.scalar_tensor_tensor(
                db[3][:, :], db[1][:, :], 1.0 / D, db[2][:, :], ALU.mult, ALU.subtract),
                waits=[(dLN, dLN.n)], inc=dLN)
            p.add("scalar", lambda h: h.activation(out=db[3][:, :], in_=db[3][:, :],
                                                   func=AF.Sqrt, bias=epsb[:, :]),
                  waits=[(dLN, dLN.n)], inc=dLN)
            p.add("vector", lambda h: h.reciprocal(db[4][:, :], db[3][:, :]),
                  waits=[(dLN, dLN.n)])
            p.add("vector", lambda h, zsrc=zsrc: h.tensor_scalar(
                outb[:, :], zsrc, db[0][:, :], db[4][:, :], ALU.subtract, ALU.mult),
                waits=[(dX1, 16 * (tc * 4 + t2 + 1))])
            p.add("vector", lambda h: h.tensor_add(outb[:, :], outb[:, :], x1t[:, :]), inc=dOC)
            p.add("sync", lambda h, row0=row0: h.dma_start(
                out=out_d.ap()[row0:row0 + 128, :], in_=outb[:, :]),
                waits=[(dOC, tc * 4 + t2 + 1)], inc=sOUT, inc_k=16)

    if d_mode == "ug0":
        hid_d = nc.dram_tensor("hidT", [128, 64 * 512], BF16, kind="ExternalOutput")
        p.add("sync", lambda h: h.dma_start(out=hid_d.ap(), in_=hidT[:, :]),
              waits=[(dHID, dHID.n)], inc=sOUT, inc_k=16)
        p.add("sync", lambda h: None, waits=[(sOUT, 16)])
    elif d_mode == "down0":
        z_d = nc.dram_tensor("zdbg", [128, 4 * D], F32, kind="ExternalOutput")
        p.add("sync", lambda h: h.dma_start(out=z_d.ap(), in_=zfull[:, :]),
              waits=[(dZE, dZE.n)], inc=sOUT, inc_k=16)
        p.add("sync", lambda h: None, waits=[(sOUT, 16)])
    else:
        p.add("sync", lambda h: None, waits=[(sOUT, 64 * d_tcs)])
    p.emit_block()
    pop_scope()
    pop_scope()
    return nc


_CACHE = {}


def _consts():
    c = {}
    c["ident"] = np.eye(128, dtype=np.float32).astype(BF)
    c["identf"] = np.eye(128, dtype=np.float32)
    c["onescol"] = np.ones((128, 1), np.float32).astype(BF)
    c["onescolf"] = np.ones((128, 1), np.float32)
    c["onesrowf"] = np.ones((1, 128), np.float32)
    m = np.zeros((128, 2048), np.float32)
    ii = np.arange(128)[:, None]
    jj = np.arange(512)[None, :]
    for dix in range(4):
        m[:, dix * 512:(dix + 1) * 512] = (dix * 128 + ii <= jj).astype(np.float32)
    c["masks"] = m.astype(BF)
    c["epsb"] = np.full((128, 1), LN_EPS, np.float32)
    return c


def kernel(L0_attn_x, L0_attn_Wq, L0_attn_Wk, L0_attn_Wv, L0_attn_Wo,
           L0_attn_lambda_q1, L0_attn_lambda_k1, L0_attn_lambda_q2, L0_attn_lambda_k2,
           L0_ffn_W_up, L0_ffn_W_gate, L0_ffn_W_down):
    x = np.ascontiguousarray(np.asarray(L0_attn_x, np.float32))
    Wq = np.ascontiguousarray(np.asarray(L0_attn_Wq, np.float32))
    Wk = np.ascontiguousarray(np.asarray(L0_attn_Wk, np.float32))
    Wv = np.ascontiguousarray(np.asarray(L0_attn_Wv, np.float32))
    Wo = np.ascontiguousarray(np.asarray(L0_attn_Wo, np.float32))
    Wu = np.ascontiguousarray(np.asarray(L0_ffn_W_up, np.float32))
    Wg = np.ascontiguousarray(np.asarray(L0_ffn_W_gate, np.float32))
    Wd = np.ascontiguousarray(np.asarray(L0_ffn_W_down, np.float32))
    lamsv = np.concatenate([
        np.asarray(L0_attn_lambda_q1, np.float32).ravel(),
        np.asarray(L0_attn_lambda_k1, np.float32).ravel(),
        np.asarray(L0_attn_lambda_q2, np.float32).ravel(),
        np.asarray(L0_attn_lambda_k2, np.float32).ravel()]).reshape(1, 256)

    if "nc" not in _CACHE:
        _CACHE["nc"] = build_nc()
    nc = _CACHE["nc"]
    consts = _consts()
    in_maps = []
    for c in range(8):
        b, hg = c // 2, c % 2
        im = {
            "x": x[b],
            "xown": np.ascontiguousarray(x[b][2048 * hg:2048 * (hg + 1)]),
            "wq": np.ascontiguousarray(Wq[:, 1024 * hg:1024 * (hg + 1)]),
            "wk": np.ascontiguousarray(Wk[:, 1024 * hg:1024 * (hg + 1)]),
            "wv": np.ascontiguousarray(Wv[:, 1024 * hg:1024 * (hg + 1)]),
            "wo": Wo, "lams": lamsv, "wu": Wu, "wg": Wg, "wd": Wd,
        }
        im.update(consts)
        in_maps.append(im)
    res = run_bass_kernel_spmd(nc, in_maps, core_ids=list(range(8)))
    out = np.empty((4, T, D), np.float32)
    for c in range(8):
        b, hg = c // 2, c % 2
        out[b, 2048 * hg:2048 * (hg + 1), :] = res.results[c]["out"]
    return out
